# version 23
# baseline (speedup 1.0000x reference)
import numpy as np
import ml_dtypes
from contextlib import ExitStack

import concourse.bass as bass
import concourse.mybir as mybir
from concourse.bass_utils import run_bass_kernel_spmd

F32 = mybir.dt.float32
BF16 = mybir.dt.bfloat16
AF = mybir.ActivationFunctionType
ALU = mybir.AluOpType
AX = mybir.AxisListType

S = 8192
E = 1024
HD = 64
H = 2048
NE = 9
NST = 8
STT = 1024
NIT = 16
NKC = 64
GRP = 4
NCORES = 8


class Sem:
    __slots__ = ("name", "val", "h")

    def __init__(self, name):
        self.name = name
        self.val = 0
        self.h = None


class Plan:
    ENGINES = ("sp", "pe", "act", "dve", "po")

    def __init__(self):
        self.steps = {e: [] for e in self.ENGINES}
        self.sems = []

    def sem(self, name):
        s = Sem(name)
        self.sems.append(s)
        return s

    def emit(self, eng, fn, waits=(), inc=None, n=1):
        ws = tuple((s, v) for (s, v) in waits if s is not None and v > 0)
        self.steps[eng].append((fn, ws, inc, n))
        if inc is not None:
            inc.val += n
            return (inc, inc.val)
        return None


def _replay(plan, name, eng):
    last = {}
    for fn, waits, inc, n in plan.steps[name]:
        for s, v in waits:
            if last.get(s.name, 0) < v:
                eng.wait_ge(s.h, v)
                last[s.name] = v
        inst = fn(eng)
        if inc is not None:
            inst.then_inc(inc.h, n)


class Lay:
    def __init__(self, start=0):
        self.off = start

    def take(self, per_part_bytes):
        o = (self.off + 31) & ~31
        self.off = o + per_part_bytes
        return o


def ew(*evs):
    return [e for e in evs if e is not None]


def build(taps=False):
    nc = bass.Bass("TRN2", target_bir_lowering=False)

    xT = nc.dram_tensor("xT", [E, S], F32, kind="ExternalInput").ap()
    xtok = nc.dram_tensor("xtok", [S, E], F32, kind="ExternalInput").ap()
    wqkvp = nc.dram_tensor("wqkvp", [128, 3 * 512], F32, kind="ExternalInput").ap()
    gwp = nc.dram_tensor("gwp", [128, 64], F32, kind="ExternalInput").ap()
    w1p = nc.dram_tensor("w1p", [NE, 128, 16384], F32, kind="ExternalInput").ap()
    w2p = nc.dram_tensor("w2p", [NE, 128, 16384], F32, kind="ExternalInput").ap()
    w3p = nc.dram_tensor("w3p", [NE, 128, 16384], F32, kind="ExternalInput").ap()
    out = nc.dram_tensor("out", [S, E], F32, kind="ExternalOutput").ap()
    rows_hbm = nc.dram_tensor("rows_hbm", [2, S], F32, kind="Internal").ap()
    tap_aps = {}
    if taps:
        for nm, shp, dt in [
            ("t_qt", [128, S], BF16), ("t_kt", [128, S], BF16),
            ("t_vb", [128, 4096], BF16), ("t_stok", [128, 4096], F32),
            ("t_xn", [128, 8192], BF16), ("t_g8", [128, 64], F32),
            ("t_wst", [128, 64], F32), ("t_ytot", [128, 8192], F32),
            ("t_rows", [2, S], F32), ("t_u", [128, 16384], BF16),
            ("t_pt0", [128, 8192], BF16), ("t_mst", [128, 32], F32),
        ]:
            tap_aps[nm] = nc.dram_tensor(nm, shp, dt, kind="ExternalOutput").ap()

    p = Plan()
    sp_d = p.sem("sp_d")
    rd = p.sem("rd")
    od2 = [p.sem("od0"), p.sem("od1")]
    xk2 = [p.sem("xk0"), p.sem("xk1")]
    wsm = p.sem("wsm")
    wx2 = [p.sem("wx0"), p.sem("wx1")]
    wx3s = p.sem("wx3")
    w1m = [p.sem("w1m0"), p.sem("w1m1")]
    w2m = [p.sem("w2m0"), p.sem("w2m1")]
    w3m = p.sem("w3m")
    pe = p.sem("pe")
    act = p.sem("act")
    dve = p.sem("dve")
    po = p.sem("po")

    with ExitStack() as ctx:
        ARENA = 206 * 1024
        arena = ctx.enter_context(nc.sbuf_tensor("arena", [128, ARENA], mybir.dt.uint8))
        base = nc.lookup_mloc(arena).addr

        def at(name, shape, dtype, off):
            per = int(np.prod(shape[1:])) * mybir.dt.size(dtype)
            assert off + per <= ARENA, (name, off, per)
            t = nc.alloc_sbuf_tensor_at(name, shape, dtype, offset=base + off)
            return t.ap()

        perm = Lay(0)
        wqkv = at("wqkv", [128, 1536], BF16, perm.take(3072))
        gwb = at("gwb", [128, 64], BF16, perm.take(128))
        ones_c = at("ones_c", [128, 1], BF16, perm.take(32))
        ones_f = at("ones_f", [33, 128], F32, perm.take(512))
        s_tok = at("s_tok", [128, 4096], F32, perm.take(16384))
        mstat = at("mstat", [128, 64], F32, perm.take(256))
        vzb = at("vzb", [128, 256], BF16, perm.take(512))
        SHARED = perm.off

        l12 = Lay(SHARED)
        xbf1 = [at(f"xbf1_{i}", [128, 4096], BF16, l12.take(8192)) for i in range(2)]
        x2b1 = at("x2b1", [128, 4096], BF16, l12.take(8192))
        xn1 = at("xn1", [128, 4096], BF16, l12.take(8192))
        ttmp1 = at("ttmp1", [128, 512], F32, l12.take(2048))
        ttmp1p = at("ttmp1p", [128, 512], F32, l12.take(2048))
        scrA2 = [at(f"scrA{i}", [1, 512], F32, l12.take(2048)) for i in range(2)]
        scrB2 = [at(f"scrB{i}", [1, 512], F32, l12.take(2048)) for i in range(2)]
        rows12 = [at(f"rows1{i}", [33, 512], F32, l12.take(2048)) for i in range(2)]
        QT = at("QT", [128, S], BF16, l12.take(16384))
        KT = at("KT", [128, S], BF16, l12.take(16384))
        Vb = at("Vb", [128, 4096], BF16, l12.take(8192))
        ksq = at("ksq", [64, S], BF16, l12.take(16384))
        bneg = at("bneg", [128, 64], F32, l12.take(256))
        kno = at("kno", [128, 64], F32, l12.take(256))
        negQ = at("negQ", [128, 1], F32, l12.take(32))
        c60 = at("c60", [128, 1], F32, l12.take(32))
        qm16 = at("qm16", [1, 16], F32, l12.take(64))
        pT = [at(f"pT{i}", [128, 8192], BF16, l12.take(16384)) for i in range(GRP)]
        assert l12.off <= ARENA, l12.off

        l3 = Lay(SHARED)
        xn3 = at("xn3", [128, 8192], BF16, l3.take(16384))
        u_buf = at("u_buf", [128, 16384], BF16, l3.take(32768))
        y_tot = at("y_tot", [128, 8192], F32, l3.take(32768))
        w3s = at("w3s", [128, 16384], BF16, l3.take(32768))
        w1s = [at(f"w1s{i}", [128, 2048], BF16, l3.take(4096)) for i in range(2)]
        w2s = [at(f"w2s{i}", [128, 2048], BF16, l3.take(4096)) for i in range(2)]
        s1b = [at(f"s1b{i}", [128, 1024], F32, l3.take(4096)) for i in range(2)]
        rows3 = at("rows3", [33, 1024], F32, l3.take(4096))
        g8 = at("g8", [128, 64], F32, l3.take(256))
        rm1 = at("rm1", [128, 8], F32, l3.take(32))
        rm2 = at("rm2", [128, 8], F32, l3.take(32))
        rs12 = at("rs12", [128, 8], F32, l3.take(32))
        rmsk = at("rmsk", [128, 64], F32, l3.take(256))
        w_st = at("w_st", [128, 64], F32, l3.take(256))
        xtk = [at(f"xtk{i}", [128, 1024], F32, l3.take(4096)) for i in range(2)]
        gtmp = [at(f"gtmp{i}", [128, 1024], F32, l3.take(4096)) for i in range(2)]
        gsb = at("gsb", [128, 1024], F32, l3.take(4096))
        abf = at("abf", [128, 1024], F32, l3.take(4096))
        bbf = at("bbf", [128, 1024], F32, l3.take(4096))
        xbfp = at("xbfp", [128, 8192], BF16, l3.take(16384))
        assert l3.off <= ARENA, l3.off

        ppA = [ctx.enter_context(nc.psum_tensor(f"ppA{i}", [128, 1024], F32))
               for i in range(4)]

        ps_w = {}
        ps_r = {}

        def mm_group(keys, mms, extra_waits=()):
            waits = list(extra_waits)
            for k in keys:
                waits += ps_r.pop(k, [])
            evt = None
            nmm = len(mms)
            for i, fn in enumerate(mms):
                w = waits if i == 0 else ()
                inc = pe if i == nmm - 1 else None
                r = p.emit("pe", fn, waits=w, inc=inc)
                if r is not None:
                    evt = r
            for k in keys:
                ps_w[k] = evt
            return evt

        def ps_read(eng, fn, keys, extra_waits=()):
            waits = list(extra_waits)
            for k in keys:
                w = ps_w.get(k)
                if w is not None:
                    waits.append(w)
            sem = {"dve": dve, "act": act, "po": po}[eng]
            evt = p.emit(eng, fn, waits=waits, inc=sem)
            for k in keys:
                ps_r.setdefault(k, []).append(evt)
            return evt

        p.emit("po", lambda e: e.dma_start(out=wqkv[:], in_=wqkvp), inc=wsm, n=16)
        p.emit("po", lambda e: e.dma_start(out=gwb[:], in_=gwp), inc=wsm, n=16)
        ev_smallw = (wsm, wsm.val)
        p.emit("dve", lambda e: e.memset(ones_c[:], 1.0), inc=dve)
        ev_ones_c = (dve, dve.val)
        p.emit("dve", lambda e: e.memset(ones_f[:], 1.0), inc=dve)
        ev_ones_f = (dve, dve.val)
        p.emit("act", lambda e: e.preload_act_table(AF.Sqrt))
        p.emit("dve", lambda e: e.memset(QT[64:128, :], 0.0), inc=dve)
        p.emit("dve", lambda e: e.memset(KT[64:128, :], 0.0), inc=dve)
        p.emit("dve", lambda e: e.memset(c60[:], 60.0), inc=dve)
        ev_pad = (dve, dve.val)

        xT3 = xT.rearrange("(c q) i -> q c i", q=128)

        ev_xuse = [None, None]
        ev_xn = None
        ev_xnp = None
        ev_ab1use = None
        ev_stat_prev = None
        ev_rowsrd_prev = [[], []]
        ev_scr_prev = [None, None]
        ev_rows_out = None
        ev_p1 = None
        def emit_xld(it):
            sl = it % 2
            src = xT3[:, :, it * 512:(it + 1) * 512]
            dst = xbf1[sl][:].rearrange("q (c i) -> q c i", c=8)
            return p.emit("po", lambda e, d=dst, s=src: e.dma_start(out=d, in_=s),
                          waits=ew(ev_xuse[sl], ev_ab1use), inc=wx2[sl], n=16)

        stA = {}
        stB = {}

        def stage_A(it):
            nonlocal ev_stat_prev
            sl = it % 2
            ev_ld = emit_xld(it)
            ev_sq = p.emit("act", lambda e, a=xbf1[sl]: e.activation(
                x2b1[:], a[:], AF.Square), waits=ew(ev_ld, ev_stat_prev),
                inc=act)
            mms = []
            for ec in range(8):
                mms.append(lambda e, a=xbf1[sl], c=ec: e.matmul(
                    ppA[1][0:1, 0:512], ones_c[:], a[:, c * 512:(c + 1) * 512],
                    start=(c == 0), stop=(c == 7)))
            for ec in range(8):
                mms.append(lambda e, c=ec: e.matmul(
                    ppA[1][0:1, 512:1024], ones_c[:], x2b1[:, c * 512:(c + 1) * 512],
                    start=(c == 0), stop=(c == 7)))
            ev_stat_prev = mm_group([(1, 0), (1, 1)], mms,
                                    extra_waits=ew(ev_ld, ev_sq, ev_ones_c))
            stA[it] = ev_ld

        def stage_B(it):
            nonlocal ev_rows_out
            sl = it % 2
            scrA = scrA2[sl]
            scrB = scrB2[sl]
            rows1 = rows12[sl]
            ev_mean = ps_read("dve", lambda e, o=scrA: e.tensor_scalar_mul(
                o[:], ppA[1][0:1, 0:512], 1.0 / E), [(1, 0)])
            ev_msq = p.emit("act", lambda e, o=scrB, i_=scrA: e.activation(
                o[:], i_[:], AF.Square), waits=ew(ev_mean), inc=act)
            ev_var = ps_read("dve", lambda e, o=scrB: e.scalar_tensor_tensor(
                o[:], ppA[1][0:1, 512:1024], 1.0 / E, o[:], ALU.mult,
                ALU.subtract), [(1, 1)], extra_waits=ew(ev_msq))
            ev_vre = p.emit("dve", lambda e, o=scrB: e.tensor_scalar_add(
                o[:], o[:], 1e-5), waits=ew(ev_var), inc=dve)
            ev_sd = p.emit("act", lambda e, o=scrB: e.activation(
                o[:], o[:], AF.Sqrt), waits=ew(ev_vre), inc=act)
            ev_a = p.emit("dve", lambda e, o=rows1, i_=scrB: e.reciprocal(
                o[0:1, :], i_[:]), waits=ew(ev_sd, *ev_rowsrd_prev[sl]),
                inc=dve)
            ev_b = p.emit("dve", lambda e, o=rows1, a_=scrA: e.scalar_tensor_tensor(
                o[32:33, :], a_[:], -1.0, o[0:1, :], ALU.mult, ALU.mult),
                waits=ew(ev_a), inc=dve)
            ev_ro0 = p.emit("sp", lambda e, t=it, o=rows1: e.dma_start(
                out=rows_hbm[0:1, t * 512:(t + 1) * 512], in_=o[0:1, :]),
                waits=ew(ev_a) + [(rd, rd.val)], inc=rd, n=16)
            ev_rows_out = p.emit("sp", lambda e, t=it, o=rows1: e.dma_start(
                out=rows_hbm[1:2, t * 512:(t + 1) * 512], in_=o[32:33, :]),
                waits=ew(ev_b) + [(rd, rd.val)], inc=rd, n=16)
            ev_ab = mm_group([(2, 0)], [lambda e, o=rows1: e.matmul(
                ppA[2][:, 0:512], ones_f[0:1, :], o[0:1, :], start=True,
                stop=True)], extra_waits=ew(ev_a, ev_ones_f))
            ev_bb = mm_group([(2, 1)], [lambda e, o=rows1: e.matmul(
                ppA[2][:, 512:1024], ones_f[32:33, :], o[32:33, :], start=True,
                stop=True)], extra_waits=ew(ev_b))
            ev_rowsrd_prev[sl] = [ev_ro0, ev_rows_out, ev_ab, ev_bb]
            ps_read("act", lambda e: e.copy(
                abf[:, 0:512], ppA[2][:, 0:512]), [(2, 0)],
                extra_waits=ew(ev_ab1use, ev_xn))
            ev_evb = ps_read("act", lambda e: e.copy(
                bbf[:, 0:512], ppA[2][:, 512:1024]), [(2, 1)])
            stB[it] = ev_evb

        def stage_C(it):
            nonlocal ev_xn, ev_xnp, ev_ab1use, ev_p1
            sl = it % 2
            ev_evb = stB[it]
            for ec in range(8):
                if ec < 3:
                    em = p.emit("dve", lambda e, a=xbf1[sl], c=ec: e.tensor_tensor(
                        ttmp1[:], a[:, c * 512:(c + 1) * 512], abf[:, 0:512],
                        ALU.mult), waits=ew(ev_evb, ev_xn), inc=dve)
                    ev_xn = p.emit("dve", lambda e, c=ec: e.tensor_tensor(
                        xn1[:, c * 512:(c + 1) * 512], ttmp1[:], bbf[:, 0:512],
                        ALU.add), waits=ew(em), inc=dve)
                else:
                    em = p.emit("po", lambda e, a=xbf1[sl], c=ec: e.tensor_tensor(
                        ttmp1p[:], a[:, c * 512:(c + 1) * 512], abf[:, 0:512],
                        ALU.mult), waits=ew(ev_evb, ev_xnp), inc=po)
                    ev_xnp = p.emit("po", lambda e, c=ec: e.tensor_tensor(
                        xn1[:, c * 512:(c + 1) * 512], ttmp1p[:], bbf[:, 0:512],
                        ALU.add), waits=ew(em), inc=po)
            ev_ab1use = (po, po.val)
            ev_xuse[sl] = ev_xn
            qmm = [lambda e, c=ec: e.matmul(
                ppA[3][0:64, 0:512], wqkv[:, c * 64:(c + 1) * 64],
                xn1[:, c * 512:(c + 1) * 512], start=(c == 0), stop=(c == 7))
                for ec in range(8)]
            kmm = [lambda e, c=ec: e.matmul(
                ppA[3][0:64, 512:1024], wqkv[:, 512 + c * 64:512 + (c + 1) * 64],
                xn1[:, c * 512:(c + 1) * 512], start=(c == 0), stop=(c == 7))
                for ec in range(8)]
            mm_group([(3, 0), (3, 1)], qmm + kmm,
                     extra_waits=ew(ev_xn, ev_xnp, ev_smallw))
            vmm = []
            for g4 in range(4):
                for ec in range(8):
                    vmm.append(lambda e, g=g4, c=ec: e.matmul(
                        ppA[0][:, g * 64:(g + 1) * 64],
                        xn1[:, c * 512 + g * 128:c * 512 + (g + 1) * 128],
                        wqkv[:, 1024 + c * 64:1024 + (c + 1) * 64],
                        start=(c == 0), stop=(c == 7)))
            mm_group([(0, 0)], vmm, extra_waits=ew(ev_xn, ev_xnp))
            ps_read("act", lambda e, t=it: e.copy(
                QT[0:64, t * 512:(t + 1) * 512], ppA[3][0:64, 0:512]), [(3, 0)])
            ps_read("act", lambda e, t=it: e.copy(
                KT[0:64, t * 512:(t + 1) * 512], ppA[3][0:64, 512:1024]), [(3, 1)])
            ev_p1a = None
            for g4 in range(4):
                ev_p1a = ps_read("act", lambda e, t=it, g=g4: e.copy(
                    Vb[:, (t * 4 + g) * 64:(t * 4 + g + 1) * 64],
                    ppA[0][:, g * 64:(g + 1) * 64]), [(0, 0)])
            ev_p1 = ev_p1a

        stage_A(0)
        stage_B(0)
        stage_A(1)
        for it in range(NIT):
            stage_C(it)
            if it + 1 < NIT:
                stage_B(it + 1)
            if it + 2 < NIT:
                stage_A(it + 2)

        if taps:
            p.emit("sp", lambda e: e.dma_start(out=tap_aps["t_qt"], in_=QT[:]),
                   waits=ew(ev_p1), inc=sp_d, n=16)
            p.emit("sp", lambda e: e.dma_start(out=tap_aps["t_kt"], in_=KT[:]),
                   waits=ew(ev_p1), inc=sp_d, n=16)
            p.emit("sp", lambda e: e.dma_start(out=tap_aps["t_vb"], in_=Vb[:]),
                   waits=ew(ev_p1), inc=sp_d, n=16)
            p.emit("sp", lambda e: e.dma_start(out=tap_aps["t_rows"], in_=rows_hbm),
                   waits=ew(ev_rows_out), inc=sp_d, n=16)

        e_ks = p.emit("act", lambda e: e.activation(ksq[:], KT[0:64, :], AF.Square),
                      waits=ew(ev_p1), inc=act)
        knmm = []
        for kc in range(NKC):
            knmm.append(lambda e, k=kc: e.matmul(
                ppA[1][:, k:k + 1], ksq[:, k * 128:(k + 1) * 128],
                ones_c[0:64, :], start=True, stop=True))
        ev_knm = mm_group([(1, 0)], knmm, extra_waits=ew(e_ks, ev_ones_c))
        e_qs = p.emit("act", lambda e: e.activation(ksq[:], QT[0:64, :], AF.Square),
                      waits=ew(ev_knm), inc=act)
        ev_qn = None
        for t in range(16):
            h = t % 2
            key = (3, h)
            evm = mm_group([key], [lambda e, tt=t, hh=h: e.matmul(
                ppA[3][0:1, hh * 512:hh * 512 + 512],
                ones_c[0:64, :], ksq[:, tt * 512:(tt + 1) * 512],
                start=True, stop=True)],
                extra_waits=ew(e_qs) if t == 0 else ())
            ev_qn = ps_read("dve", lambda e, tt=t, hh=h: e.reduce_max(
                qm16[:, tt:tt + 1], ppA[3][0:1, hh * 512:hh * 512 + 512], AX.X),
                [key])
        e_qm = p.emit("dve", lambda e: e.reduce_max(qm16[:, 0:1], qm16[:], AX.X),
                      waits=ew(ev_qn), inc=dve)
        e_qsr = p.emit("act", lambda e: e.activation(qm16[:, 0:1], qm16[:, 0:1],
                                                     AF.Sqrt),
                       waits=ew(e_qm), inc=act)
        e_qb = mm_group([(2, 0)], [lambda e: e.matmul(
            ppA[2][:, 0:1], ones_f[0:1, :], qm16[:, 0:1], start=True, stop=True)],
            extra_waits=ew(e_qsr, ev_ones_f))
        e_nq = ps_read("dve", lambda e: e.tensor_scalar_mul(
            negQ[:], ppA[2][:, 0:1], -1.0), [(2, 0)])
        e_kn = ps_read("act", lambda e: e.activation(kno[:], ppA[1][:, 0:64],
                                                     AF.Sqrt), [(1, 0)],
                       extra_waits=ew())
        e_bn = p.emit("dve", lambda e: e.scalar_tensor_tensor(
            bneg[:], kno[:], negQ[:], c60[:].broadcast_to([128, 64]),
            ALU.mult, ALU.add), waits=ew(e_kn, e_nq, ev_pad), inc=dve)

        p.emit("act", lambda e: e.preload_act_table(AF.Exp))
        rot3 = [0, 1, 3]
        ev_exp = [None] * GRP
        ev_zuse = [None] * GRP
        ev_vzuse = None
        ev_sacc = {}
        pss_ap = [ppA[2][:, 0:256], ppA[2][:, 512:768]]
        rot = 0
        pending = None
        ptuse_fine = None

        def emit_burst(pend):
            nonlocal ptuse_fine, ev_vzuse
            ev_expg, ev_vzg, g_ = pend
            gw_ = ew(ev_vzg, *ev_expg)
            fine = [None] * 16
            for ig4 in range(16):
                slot = ig4 % 2
                key = (2, slot)
                for sub in range(4):
                    ig = ig4 * 4 + sub
                    pp = pss_ap[slot][:, sub * 64:(sub + 1) * 64]
                    mms = [lambda e, cc=c2, i=ig, pp_=pp: e.matmul(
                        pp_, pT[cc][:, i * 128:(i + 1) * 128],
                        vzb[:, cc * 64:(cc + 1) * 64], start=(cc == 0),
                        stop=(cc == GRP - 1)) for c2 in range(GRP)]
                    if sub == 0:
                        waits = list(gw_) + ps_r.pop(key, [])
                    else:
                        waits = []
                    for i2, fn2 in enumerate(mms):
                        r2 = p.emit("pe", fn2, waits=waits if i2 == 0 else (),
                                    inc=pe if (sub == 3 and i2 == GRP - 1) else None)
                        if r2 is not None:
                            ps_w[key] = r2
                            fine[ig4] = r2
                sl4 = s_tok[:, ig4 * 256:(ig4 + 1) * 256]
                if g_ == 0:
                    ev_sacc[ig4] = ps_read("dve", lambda e, o=sl4, s_=slot:
                                           e.tensor_copy(o, pss_ap[s_]), [key])
                else:
                    ev_sacc[ig4] = ps_read("dve", lambda e, o=sl4, s_=slot:
                                           e.tensor_tensor(o, pss_ap[s_], o,
                                                           ALU.add), [key])
            ptuse_fine = fine
            ev_vzuse = (pe, pe.val)

        for g in range(NKC // GRP):
            e_zc = [None] * GRP
            for c in range(GRP):
                kc = g * GRP + c
                bc = bneg[:, kc:kc + 1]
                pair_keys = [None] * 8
                for tp in range(8):
                    r = rot3[rot % 3]
                    rot += 1
                    keys = [(r, 0), (r, 1)]
                    pair_keys[tp] = (keys, r)
                nemit = 0
                for tp in range(3):
                    keys, r = pair_keys[tp]
                    mms = [lambda e, k=kc, tt=2 * tp + hh, rr=r, hh2=hh: e.matmul(
                        ppA[rr][:, hh2 * 512:(hh2 + 1) * 512],
                        KT[:, k * 128:(k + 1) * 128],
                        QT[:, tt * 512:(tt + 1) * 512], start=True, stop=True)
                        for hh in range(2)]
                    mm_group(keys, mms, extra_waits=ew(ev_p1))
                    nemit = 3
                if c == 0 and pending is not None:
                    emit_burst(pending)
                    pending = None
                e_x = None
                for tp in range(8):
                    keys, r = pair_keys[tp]
                    if tp >= nemit:
                        mms = [lambda e, k=kc, tt=2 * tp + hh, rr=r, hh2=hh:
                               e.matmul(
                            ppA[rr][:, hh2 * 512:(hh2 + 1) * 512],
                            KT[:, k * 128:(k + 1) * 128],
                            QT[:, tt * 512:(tt + 1) * 512], start=True,
                            stop=True) for hh in range(2)]
                        mm_group(keys, mms, extra_waits=ew(ev_p1))
                    war = (ptuse_fine[2 * tp + 1] if ptuse_fine is not None
                           else None)
                    e_x = ps_read("act", lambda e, cc=c, tp_=tp, rr=r, b_=bc,
                                  z_=mstat[:, c * 8 + tp:c * 8 + tp + 1]:
                                  e.activation(
                        pT[cc][:, tp_ * 1024:(tp_ + 1) * 1024], ppA[rr][:],
                        AF.Exp, bias=b_, accum_out=z_), keys,
                        extra_waits=ew(e_bn, war,
                                       ev_zuse[c] if tp == 0 else None))
                ev_exp[c] = e_x
                e_zc[c] = p.emit("dve", lambda e, cc=c: e.reduce_sum(
                    mstat[:, 32 + cc:32 + cc + 1],
                    mstat[:, cc * 8:cc * 8 + 8], AX.X), waits=ew(e_x), inc=dve)
                ev_zuse[c] = e_zc[c]
            e_zi = p.emit("dve", lambda e: e.reciprocal(
                mstat[:, 36:40], mstat[:, 32:36]), waits=ew(*e_zc), inc=dve)
            ev_vz = p.emit("po", lambda e, gg=g: e.tensor_tensor(
                vzb[:].rearrange("q (a b) -> q a b", b=64),
                Vb[:, gg * 256:(gg + 1) * 256].rearrange("q (a b) -> q a b", b=64),
                mstat[:, 36:40].unsqueeze(2).broadcast_to([128, 4, 64]), ALU.mult),
                waits=ew(e_zi, ev_vzuse), inc=po)
            pending = (list(ev_exp), ev_vz, g)
        emit_burst(pending)
        ev_p2_pe = (pe, pe.val)
        ev_p2_act = (act, act.val)
        if taps:
            p.emit("sp", lambda e: e.dma_start(out=tap_aps["t_stok"], in_=s_tok[:]),
                   waits=ew(ev_sacc[15]), inc=sp_d, n=16)
            p.emit("sp", lambda e: e.dma_start(out=tap_aps["t_pt0"], in_=pT[0][:]),
                   waits=ew(ev_p2_pe), inc=sp_d, n=16)
            p.emit("sp", lambda e: e.dma_start(out=tap_aps["t_mst"], in_=mstat[:, 0:32]),
                   waits=ew(ev_p2_pe), inc=sp_d, n=16)

        p.emit("act", lambda e: e.preload_act_table(AF.Silu))
        NPIECE = NST * NE * 8
        piece_ev = {}
        piece_cons = {}

        def emit_piece(n):
            if n >= NPIECE:
                return
            st_, rem = divmod(n, NE * 8)
            e_, q_ = divmod(rem, 8)
            b_ = n % 2
            w_ = ew(piece_cons.get(n - 2, ev_p2_pe))
            e1 = p.emit("po", lambda e, E_=e_, qq=q_, bb=b_: e.dma_start(
                out=w1s[bb][:], in_=w1p[E_, :, qq * 2048:(qq + 1) * 2048]),
                waits=w_, inc=w1m[b_], n=16)
            e2 = p.emit("po", lambda e, E_=e_, qq=q_, bb=b_: e.dma_start(
                out=w2s[bb][:], in_=w2p[E_, :, qq * 2048:(qq + 1) * 2048]),
                inc=w2m[b_], n=16)
            piece_ev[n] = (e1, e2)

        emit_piece(0)
        emit_piece(1)
        ev_w3use = ev_p2_act
        ev_w3use2 = ev_p2_pe
        ev_uuse = ev_p2_pe
        ev_xnuse = None
        ev_rows3use = ev_p2_pe
        ev_s1use = [ev_p2_pe, ev_p2_pe]
        ev_wstuse = ev_p2_pe
        ev_g8use = ev_p2_pe
        ev_xtkuse = [ev_p2_pe, ev_p2_pe]
        ev_gtmpuse = [ev_p2_pe, ev_p2_pe]
        ev_gsuse = ev_p2_pe
        ev_stuse = [None, None]
        ev_adprev = None
        ev_ytuse = {}
        ev_yevac = {}
        ev_w = None
        ev_xbfuse = None
        ev_xbload = None
        ev_abuse = ev_ab1use
        ev_xn3 = None
        gpre = 0

        def emit_prelude(st, defer=False):
            nonlocal ev_rows3use, ev_xn3, ev_xbfuse, ev_abuse
            ev_r3a = p.emit("sp", lambda e, t=st: e.dma_start(
                out=rows3[0:1, :], in_=rows_hbm[0:1, t * 1024:(t + 1) * 1024]),
                waits=ew(ev_rows_out, ev_rows3use) + [(rd, rd.val)], inc=rd, n=16)
            ev_r3b = p.emit("sp", lambda e, t=st: e.dma_start(
                out=rows3[32:33, :], in_=rows_hbm[1:2, t * 1024:(t + 1) * 1024]),
                waits=[(rd, rd.val)], inc=rd, n=16)
            mm_group([(0, 0), (0, 1)], [
                lambda e, h=h2: e.matmul(
                    ppA[0][:, h * 512:(h + 1) * 512], ones_f[0:1, :],
                    rows3[0:1, h * 512:(h + 1) * 512], start=True, stop=True)
                for h2 in range(2)], extra_waits=ew(ev_r3a, ev_ones_f))
            mm_group([(1, 0), (1, 1)], [
                lambda e, h=h2: e.matmul(
                    ppA[1][:, h * 512:(h + 1) * 512], ones_f[32:33, :],
                    rows3[32:33, h * 512:(h + 1) * 512], start=True, stop=True)
                for h2 in range(2)], extra_waits=ew(ev_r3b))
            ev_rows3use = (pe, pe.val)
            ps_read("dve", lambda e: e.tensor_copy(abf[:], ppA[0][:]),
                    [(0, 0), (0, 1)], extra_waits=ew(ev_abuse))
            ev_ab2 = ps_read("dve", lambda e: e.tensor_copy(bbf[:], ppA[1][:]),
                             [(1, 0), (1, 1)])
            tts = []
            for ec in range(8):
                tts.append((lambda e, c=ec: e.tensor_tensor(
                    s1b[0][:], xbfp[:, c * 1024:(c + 1) * 1024], abf[:],
                    ALU.mult),
                    ew(ev_ab2, ev_s1use[0], ev_xnuse, ev_xbload) if ec == 0
                    else (), True))
                tts.append((lambda e, c=ec: e.tensor_tensor(
                    xn3[:, c * 1024:(c + 1) * 1024], s1b[0][:], bbf[:],
                    ALU.add), (), False))
            if defer:
                return tts
            ev_xn = None
            for fn, w_, _m in tts:
                ev_xn = p.emit("dve", fn, waits=list(w_) + ew(ev_xn), inc=dve)
            ev_s1use[0] = ev_xn
            ev_xn3 = ev_xn
            ev_xbfuse = ev_xn
            ev_abuse = ev_xn
            return None

        def finish_prelude(ev_xn):
            nonlocal ev_s1use, ev_xn3, ev_xbfuse, ev_abuse
            ev_s1use[0] = ev_xn
            ev_xn3 = ev_xn
            ev_xbfuse = ev_xn
            ev_abuse = ev_xn

        def emit_xbfp_load(st):
            nonlocal ev_xbload
            src2 = xT3[:, :, st * 1024:(st + 1) * 1024]
            dst2 = xbfp[:].rearrange("q (c i) -> q c i", c=8)
            ev_xbload = p.emit("po", lambda e, d=dst2, s=src2: e.dma_start(
                out=d, in_=s), waits=ew(ev_xbfuse), inc=wx3s, n=16)
            return ev_xbload

        ev_xbfuse = None
        emit_xbfp_load(0)
        emit_prelude(0)

        for st in range(NST):
            gmm = []
            for ig in range(8):
                for ec in range(8):
                    gmm.append(lambda e, i=ig, c=ec: e.matmul(
                        ppA[2][:, i * 8:(i + 1) * 8],
                        xn3[:, c * 1024 + i * 128:c * 1024 + (i + 1) * 128],
                        gwb[:, c * 8:(c + 1) * 8], start=(c == 0), stop=(c == 7)))
            mm_group([(2, 0)], gmm, extra_waits=ew(ev_xn3, ev_smallw))
            ev_g8 = ps_read("dve", lambda e: e.tensor_copy(g8[:], ppA[2][:, 0:64]),
                            [(2, 0)], extra_waits=ew(ev_g8use, ev_wstuse))
            g83 = g8[:].rearrange("q (a b) -> q a b", b=8)
            rmsk3 = rmsk[:].rearrange("q (a b) -> q a b", b=8)
            wst3 = w_st[:].rearrange("q (a b) -> q a b", b=8)
            e_m1 = p.emit("dve", lambda e: e.reduce_max(rm1[:], g83, AX.X),
                          waits=ew(ev_g8), inc=dve)
            e_msk = p.emit("dve", lambda e: e.tensor_tensor(
                rmsk3, g83, rm1[:].unsqueeze(2).broadcast_to([128, 8, 8]),
                ALU.is_ge), waits=ew(e_m1), inc=dve)
            e_msk2 = p.emit("dve", lambda e: e.scalar_tensor_tensor(
                rmsk[:], rmsk[:], -1e30, g8[:], ALU.mult, ALU.add),
                waits=ew(e_msk), inc=dve)
            e_m2 = p.emit("dve", lambda e: e.reduce_max(rm2[:], rmsk3, AX.X),
                          waits=ew(e_msk2), inc=dve)
            e_s12 = p.emit("dve", lambda e: e.tensor_tensor(
                rs12[:], rm1[:], rm2[:], ALU.add), waits=ew(e_m2), inc=dve)
            e_t = p.emit("dve", lambda e: e.scalar_tensor_tensor(
                rmsk3, g83, 2.0,
                rs12[:].unsqueeze(2).broadcast_to([128, 8, 8]),
                ALU.mult, ALU.subtract), waits=ew(e_s12), inc=dve)
            e_sg = p.emit("act", lambda e: e.activation(
                rmsk[:], rmsk[:], AF.Tanh, scale=0.5), waits=ew(e_t), inc=act)
            e_sg2 = p.emit("dve", lambda e: e.tensor_scalar(
                rmsk[:], rmsk[:], 0.5, 0.5, ALU.mult, ALU.add),
                waits=ew(e_sg), inc=dve)
            e_ch = p.emit("dve", lambda e: e.tensor_tensor(
                wst3, g83, rm2[:].unsqueeze(2).broadcast_to([128, 8, 8]),
                ALU.is_ge), waits=ew(e_sg2), inc=dve)
            ev_w = p.emit("dve", lambda e: e.tensor_tensor(
                w_st[:], w_st[:], rmsk[:], ALU.mult), waits=ew(e_ch), inc=dve)
            ev_g8use = ev_w
            if taps and st == 0:
                p.emit("sp", lambda e: e.dma_start(out=tap_aps["t_g8"], in_=g8[:]),
                       waits=ew(ev_g8), inc=sp_d, n=16)
                p.emit("sp", lambda e: e.dma_start(out=tap_aps["t_wst"], in_=w_st[:]),
                       waits=ew(ev_w), inc=sp_d, n=16)
                p.emit("sp", lambda e: e.dma_start(out=tap_aps["t_xn"], in_=xn3[:]),
                       waits=ew(ev_xn3), inc=sp_d, n=16)
            if st + 1 < NST:
                emit_xbfp_load(st + 1)

            for ee in range(NE):
                ev_w3ld = None
                for q in range(4):
                    ev_w3ld = p.emit("po", lambda e, E_=ee, qq=q: e.dma_start(
                        out=w3s[:, qq * 4096:(qq + 1) * 4096],
                        in_=w3p[E_, :, qq * 4096:(qq + 1) * 4096]),
                        waits=ew(ev_w3use, ev_w3use2), inc=w3m, n=16)
                ev_w3use = None
                ev_utt = {}
                for j in range(16):
                    jp = j % 2
                    qq = j // 2
                    npc = gpre + qq
                    b = npc % 2
                    t1, t2 = 2 * jp, 2 * jp + 1
                    extra = ew(*piece_ev[npc])
                    if ee == 0 and j == 0:
                        extra += ew(ev_xn3)
                    mms = []
                    for ec in range(8):
                        for ih in range(2):
                            mms.append(lambda e, c=ec, h=ih, t=t1, jj=j, bb=b:
                                       e.matmul(
                                ppA[t][:, h * 512:(h + 1) * 512],
                                w1s[bb][:, (jj % 2) * 1024 + c * 128:
                                         (jj % 2) * 1024 + (c + 1) * 128],
                                xn3[:, c * 1024 + h * 512:c * 1024 + (h + 1) * 512],
                                start=(c == 0), stop=(c == 7)))
                    mm_group([(t1, 0), (t1, 1)], mms, extra_waits=extra)
                    mms = []
                    for ec in range(8):
                        for ih in range(2):
                            mms.append(lambda e, c=ec, h=ih, t=t2, jj=j, bb=b:
                                       e.matmul(
                                ppA[t][:, h * 512:(h + 1) * 512],
                                w2s[bb][:, (jj % 2) * 1024 + c * 128:
                                         (jj % 2) * 1024 + (c + 1) * 128],
                                xn3[:, c * 1024 + h * 512:c * 1024 + (h + 1) * 512],
                                start=(c == 0), stop=(c == 7)))
                    mm_group([(t2, 0), (t2, 1)], mms)
                    if j == 15:
                        ev_xnuse = (pe, pe.val)
                    if j % 2 == 1:
                        piece_cons[npc] = (pe, pe.val)
                        emit_piece(npc + 2)
                    ev_s1 = p.emit("act", lambda e, jp_=jp, t=t1: e.activation(
                        s1b[jp_][:], ppA[t][:], AF.Silu),
                        waits=ew(ps_w.get((t1, 0)), ev_s1use[jp]), inc=act)
                    ev_utt[j] = p.emit("dve", lambda e, jj=j, jp_=jp, t=t2:
                                       e.tensor_tensor(
                        u_buf[:, jj * 1024:(jj + 1) * 1024], s1b[jp_][:],
                        ppA[t][:], ALU.mult),
                        waits=ew(ps_w.get((t2, 0)), ev_s1), inc=dve)
                    for k_ in ((t1, 0), (t1, 1), (t2, 0), (t2, 1)):
                        ps_r.setdefault(k_, []).append(ev_utt[j])
                    ev_s1use[jp] = ev_utt[j]
                gpre += 8
                ptts = None
                ev_xnl = None
                if ee == NE - 1 and st + 1 < NST:
                    ptts = emit_prelude(st + 1, defer=True)
                for ig in range(8):
                    t = ig % 4
                    keysB = [(t, 0), (t, 1)]
                    bw = list(ew(ev_w3ld))
                    for k in keysB:
                        bw += ps_r.pop(k, [])
                    evB = None
                    for j in range(16):
                        for eh in range(2):
                            w_ = (bw if (j == 0 and eh == 0) else
                                  (ew(ev_utt[j]) if (ig == 0 and eh == 0)
                                   else ()))
                            if j == 0 and eh == 0:
                                w_ = bw + ew(ev_utt[0])
                            r3 = p.emit("pe", lambda e, jj=j, h=eh, t_=t, i=ig:
                                        e.matmul(
                                ppA[t_][:, h * 512:(h + 1) * 512],
                                u_buf[:, jj * 1024 + i * 128:
                                      jj * 1024 + (i + 1) * 128],
                                w3s[:, jj * 1024 + h * 512:
                                    jj * 1024 + (h + 1) * 512],
                                start=(jj == 0), stop=(jj == 15)), waits=w_,
                                inc=pe if (j == 15 and eh == 1) else None)
                            if r3 is not None:
                                evB = r3
                    for k in keysB:
                        ps_w[k] = evB
                    if ig == 7:
                        ev_uuse = (pe, pe.val)
                        ev_w3use2 = (pe, pe.val)
                    for eh in range(2):
                        ysl = y_tot[:, ig * 1024 + eh * 512:
                                    ig * 1024 + (eh + 1) * 512]
                        yps = ppA[t][:, eh * 512:(eh + 1) * 512]
                        wcol = (w_st[:, ig * 8 + ee:ig * 8 + ee + 1]
                                if ee < 8 else None)
                        if ee == 0:
                            ev_yevac[(ig, eh)] = ps_read(
                                "dve", lambda e, o=ysl, i_=yps, w=wcol:
                                e.tensor_scalar_mul(o, i_, w), [(t, eh)],
                                extra_waits=ew(ev_w, ev_ytuse.get(ig)))
                        elif ee == NE - 1:
                            ev_yevac[(ig, eh)] = ps_read(
                                "dve", lambda e, o=ysl, i_=yps:
                                e.scalar_tensor_tensor(o, i_, 0.0, o, ALU.bypass,
                                                       ALU.add), [(t, eh)])
                        else:
                            ev_yevac[(ig, eh)] = ps_read(
                                "dve", lambda e, o=ysl, i_=yps, w=wcol:
                                e.scalar_tensor_tensor(o, i_, w, o, ALU.mult,
                                                       ALU.add), [(t, eh)],
                                extra_waits=ew(ev_w))
                    if ptts is not None:
                        for fn, w_, _m in ptts[ig * 2:ig * 2 + 2]:
                            ev_xnl = p.emit("dve", fn,
                                            waits=list(w_) + ew(ev_xnl),
                                            inc=dve)
                        if ig == 7:
                            finish_prelude(ev_xnl)
                    if ee == NE - 1:
                        sl = ig % 2
                        gig = st * 8 + ig
                        xsrc = xtok.rearrange("(g q) d -> g q d", q=128)[gig]
                        odst = out.rearrange("(g q) d -> g q d", q=128)[gig]
                        ev_xt = p.emit("sp", lambda e, s_=sl, src_=xsrc:
                                       e.dma_start(out=xtk[s_][:], in_=src_),
                                       waits=ew(ev_xtkuse[sl]), inc=xk2[sl], n=16)
                        e_si = p.emit("act", lambda e, i=ig, s_=sl: e.activation(
                            gtmp[s_][:], y_tot[:, i * 1024:(i + 1) * 1024],
                            AF.Silu),
                            waits=ew(ev_yevac[(ig, 0)], ev_yevac[(ig, 1)],
                                     ev_gtmpuse[sl], ev_stuse[sl]), inc=act)
                        ev_ytuse[ig] = e_si
                        e_gs = p.emit("dve", lambda e, g_=gig, s_=sl:
                                      e.tensor_tensor(
                            gsb[:].rearrange("q (r d) -> q r d", d=64),
                            gtmp[s_][:].rearrange("q (r d) -> q r d", d=64),
                            s_tok[:, g_ * 64:(g_ + 1) * 64].unsqueeze(1)
                            .broadcast_to([128, 16, 64]), ALU.mult),
                            waits=ew(e_si, ev_adprev,
                                     ev_sacc.get(gig // 4)
                                     if st == 0 else None), inc=dve)
                        ev_gtmpuse[sl] = e_gs
                        e_ad = p.emit("dve", lambda e, s_=sl: e.tensor_tensor(
                            gtmp[s_][:], gsb[:], xtk[s_][:], ALU.add),
                            waits=ew(ev_xt, e_gs), inc=dve)
                        ev_xtkuse[sl] = e_ad
                        ev_adprev = e_ad
                        ev_st2 = p.emit("sp", lambda e, s_=sl, od_=odst:
                                        e.dma_start(out=od_, in_=gtmp[s_][:]),
                                        waits=ew(e_ad), inc=od2[sl], n=16)
                        ev_stuse[sl] = ev_st2
                ev_wstuse = (dve, dve.val)
            if taps and st == 0:
                p.emit("sp", lambda e: e.dma_start(out=tap_aps["t_ytot"],
                                                   in_=y_tot[:]),
                       waits=ew(ev_yevac[(7, 1)]), inc=sp_d, n=16)
                p.emit("sp", lambda e: e.dma_start(out=tap_aps["t_u"], in_=u_buf[:]),
                       waits=ew(ev_uuse), inc=sp_d, n=16)

        p.emit("sp", lambda e: e.nop(),
               waits=[(od2[0], od2[0].val), (od2[1], od2[1].val),
                      (rd, rd.val), (sp_d, sp_d.val)])

        for s in p.sems:
            s.h = ctx.enter_context(nc.semaphore(s.name))
        block = ctx.enter_context(nc.Block())
        block.sync(lambda e: _replay(p, "sp", e))
        block.tensor(lambda e: _replay(p, "pe", e))
        block.scalar(lambda e: _replay(p, "act", e))
        block.vector(lambda e: _replay(p, "dve", e))
        block.gpsimd(lambda e: _replay(p, "po", e))
    return nc


def _pack_weights(ew1, ew2, ew3, sw1, sw2, sw3):
    w1 = np.concatenate([ew1, sw1[None]], 0)
    w2 = np.concatenate([ew2, sw2[None]], 0)
    w3 = np.concatenate([ew3, sw3[None]], 0)
    w1p = np.ascontiguousarray(
        w1.reshape(NE, 8, 128, 16, 128).transpose(0, 2, 3, 1, 4).reshape(
            NE, 128, 16384), dtype=np.float32)
    w2p = np.ascontiguousarray(
        w2.reshape(NE, 8, 128, 16, 128).transpose(0, 2, 3, 1, 4).reshape(
            NE, 128, 16384), dtype=np.float32)
    w3p = np.ascontiguousarray(
        w3.reshape(NE, 16, 128, 1024).transpose(0, 2, 1, 3).reshape(
            NE, 128, 16384), dtype=np.float32)
    return w1p, w2p, w3p


def _pack_proj(Wq, Wk, Wv, gate_w):
    def pk(w):
        return w.T.reshape(8, 128, 64).transpose(1, 0, 2).reshape(128, 512)
    wqkvp = np.ascontiguousarray(
        np.concatenate([pk(Wq), pk(Wk), pk(Wv)], axis=1), dtype=np.float32)
    gwp = np.ascontiguousarray(
        gate_w.reshape(8, 128, 8).transpose(1, 0, 2).reshape(128, 64),
        dtype=np.float32)
    return wqkvp, gwp


def kernel(L0_attn_x, L0_attn_Wq, L0_attn_Wk, L0_attn_Wv, L0_ffn_gate_w,
           L0_ffn_ew1, L0_ffn_ew2, L0_ffn_ew3, L0_ffn_sw1, L0_ffn_sw2,
           L0_ffn_sw3):
    x = np.asarray(L0_attn_x, dtype=np.float32)
    w1p, w2p, w3p = _pack_weights(
        np.asarray(L0_ffn_ew1, np.float32), np.asarray(L0_ffn_ew2, np.float32),
        np.asarray(L0_ffn_ew3, np.float32), np.asarray(L0_ffn_sw1, np.float32),
        np.asarray(L0_ffn_sw2, np.float32), np.asarray(L0_ffn_sw3, np.float32))
    wqkvp, gwp = _pack_proj(
        np.asarray(L0_attn_Wq, np.float32), np.asarray(L0_attn_Wk, np.float32),
        np.asarray(L0_attn_Wv, np.float32), np.asarray(L0_ffn_gate_w, np.float32))

    nc = build(taps=False)
    in_maps = []
    for b in range(NCORES):
        in_maps.append({
            "xT": np.ascontiguousarray(x[b].T),
            "xtok": np.ascontiguousarray(x[b]),
            "wqkvp": wqkvp, "gwp": gwp,
            "w1p": w1p, "w2p": w2p, "w3p": w3p,
        })
    res = run_bass_kernel_spmd(nc, in_maps, core_ids=list(range(NCORES)))
    return np.stack([np.asarray(r["out"]) for r in res.results], axis=0)


# version 24
# speedup vs baseline: 1.1805x; 1.1805x over previous
import numpy as np
import ml_dtypes
from contextlib import ExitStack

import concourse.bass as bass
import concourse.mybir as mybir
from concourse.bass_utils import run_bass_kernel_spmd

F32 = mybir.dt.float32
BF16 = mybir.dt.bfloat16
AF = mybir.ActivationFunctionType
ALU = mybir.AluOpType
AX = mybir.AxisListType

S = 8192
E = 1024
HD = 64
H = 2048
NE = 9
NST = 8
STT = 1024
NIT = 16
NKC = 64
GRP = 4
NCORES = 8


class Sem:
    __slots__ = ("name", "val", "h")

    def __init__(self, name):
        self.name = name
        self.val = 0
        self.h = None


class Plan:
    ENGINES = ("sp", "pe", "act", "dve", "po")

    def __init__(self):
        self.steps = {e: [] for e in self.ENGINES}
        self.sems = []

    def sem(self, name):
        s = Sem(name)
        self.sems.append(s)
        return s

    def emit(self, eng, fn, waits=(), inc=None, n=1):
        ws = tuple((s, v) for (s, v) in waits if s is not None and v > 0)
        self.steps[eng].append((fn, ws, inc, n))
        if inc is not None:
            inc.val += n
            return (inc, inc.val)
        return None


def _replay(plan, name, eng):
    last = {}
    for fn, waits, inc, n in plan.steps[name]:
        for s, v in waits:
            if last.get(s.name, 0) < v:
                eng.wait_ge(s.h, v)
                last[s.name] = v
        inst = fn(eng)
        if inc is not None:
            inst.then_inc(inc.h, n)


class Lay:
    def __init__(self, start=0):
        self.off = start

    def take(self, per_part_bytes):
        o = (self.off + 31) & ~31
        self.off = o + per_part_bytes
        return o


def ew(*evs):
    return [e for e in evs if e is not None]


def build(taps=False):
    nc = bass.Bass("TRN2", target_bir_lowering=False)

    xT = nc.dram_tensor("xT", [E, S], F32, kind="ExternalInput").ap()
    xtok = nc.dram_tensor("xtok", [S, E], F32, kind="ExternalInput").ap()
    wqkvp = nc.dram_tensor("wqkvp", [128, 3 * 512], F32, kind="ExternalInput").ap()
    gwp = nc.dram_tensor("gwp", [128, 64], F32, kind="ExternalInput").ap()
    w1p = nc.dram_tensor("w1p", [NE, 128, 16384], F32, kind="ExternalInput").ap()
    w2p = nc.dram_tensor("w2p", [NE, 128, 16384], F32, kind="ExternalInput").ap()
    w3p = nc.dram_tensor("w3p", [NE, 128, 16384], F32, kind="ExternalInput").ap()
    out = nc.dram_tensor("out", [S, E], F32, kind="ExternalOutput").ap()
    rows_hbm = nc.dram_tensor("rows_hbm", [2, S], F32, kind="Internal").ap()
    tap_aps = {}
    if taps:
        for nm, shp, dt in [
            ("t_qt", [128, S], BF16), ("t_kt", [128, S], BF16),
            ("t_vb", [128, 4096], BF16), ("t_stok", [128, 4096], F32),
            ("t_xn", [128, 8192], BF16), ("t_g8", [128, 64], F32),
            ("t_wst", [128, 64], F32), ("t_ytot", [128, 8192], F32),
            ("t_rows", [2, S], F32), ("t_u", [128, 16384], BF16),
            ("t_pt0", [128, 8192], BF16), ("t_mst", [128, 32], F32),
        ]:
            tap_aps[nm] = nc.dram_tensor(nm, shp, dt, kind="ExternalOutput").ap()

    p = Plan()
    sp_d = p.sem("sp_d")
    rd = p.sem("rd")
    od2 = [p.sem("od0"), p.sem("od1")]
    xk2 = [p.sem("xk0"), p.sem("xk1")]
    wsm = p.sem("wsm")
    wx2 = [p.sem("wx0"), p.sem("wx1")]
    wx3s = p.sem("wx3")
    w1m = [p.sem("w1m0"), p.sem("w1m1")]
    w2m = [p.sem("w2m0"), p.sem("w2m1")]
    w3m = p.sem("w3m")
    pe = p.sem("pe")
    act = p.sem("act")
    dve = p.sem("dve")
    po = p.sem("po")

    with ExitStack() as ctx:
        ARENA = 206 * 1024
        arena = ctx.enter_context(nc.sbuf_tensor("arena", [128, ARENA], mybir.dt.uint8))
        base = nc.lookup_mloc(arena).addr

        def at(name, shape, dtype, off):
            per = int(np.prod(shape[1:])) * mybir.dt.size(dtype)
            assert off + per <= ARENA, (name, off, per)
            t = nc.alloc_sbuf_tensor_at(name, shape, dtype, offset=base + off)
            return t.ap()

        perm = Lay(0)
        wqkv = at("wqkv", [128, 1536], BF16, perm.take(3072))
        gwb = at("gwb", [128, 64], BF16, perm.take(128))
        ones_c = at("ones_c", [128, 1], BF16, perm.take(32))
        ones_f = at("ones_f", [33, 128], F32, perm.take(512))
        s_tok = at("s_tok", [128, 4096], F32, perm.take(16384))
        mstat = at("mstat", [128, 64], F32, perm.take(256))
        vzb = at("vzb", [128, 256], BF16, perm.take(512))
        SHARED = perm.off

        l12 = Lay(SHARED)
        xbf1 = [at(f"xbf1_{i}", [128, 4096], BF16, l12.take(8192)) for i in range(2)]
        x2b1 = at("x2b1", [128, 4096], BF16, l12.take(8192))
        xn1 = at("xn1", [128, 4096], BF16, l12.take(8192))
        ttmp1 = at("ttmp1", [128, 512], F32, l12.take(2048))
        ttmp1p = at("ttmp1p", [128, 512], F32, l12.take(2048))
        scrA2 = [at(f"scrA{i}", [1, 512], F32, l12.take(2048)) for i in range(2)]
        scrB2 = [at(f"scrB{i}", [1, 512], F32, l12.take(2048)) for i in range(2)]
        rows12 = [at(f"rows1{i}", [33, 512], F32, l12.take(2048)) for i in range(2)]
        QT = at("QT", [128, S], BF16, l12.take(16384))
        KT = at("KT", [128, S], BF16, l12.take(16384))
        Vb = at("Vb", [128, 4096], BF16, l12.take(8192))
        ksq = at("ksq", [64, S], BF16, l12.take(16384))
        bneg = at("bneg", [128, 64], F32, l12.take(256))
        kno = at("kno", [128, 64], F32, l12.take(256))
        negQ = at("negQ", [128, 1], F32, l12.take(32))
        c60 = at("c60", [128, 1], F32, l12.take(32))
        qm16 = at("qm16", [1, 16], F32, l12.take(64))
        pT = [at(f"pT{i}", [128, 8192], BF16, l12.take(16384)) for i in range(GRP)]
        assert l12.off <= ARENA, l12.off

        l3 = Lay(SHARED)
        xn3 = at("xn3", [128, 8192], BF16, l3.take(16384))
        u_buf = at("u_buf", [128, 16384], BF16, l3.take(32768))
        y_tot = at("y_tot", [128, 8192], F32, l3.take(32768))
        w3s = at("w3s", [128, 16384], BF16, l3.take(32768))
        w1s = [at(f"w1s{i}", [128, 2048], BF16, l3.take(4096)) for i in range(2)]
        w2s = [at(f"w2s{i}", [128, 2048], BF16, l3.take(4096)) for i in range(2)]
        s1b = [at(f"s1b{i}", [128, 1024], F32, l3.take(4096)) for i in range(2)]
        rows3 = at("rows3", [33, 1024], F32, l3.take(4096))
        g8 = at("g8", [128, 64], F32, l3.take(256))
        rm1 = at("rm1", [128, 8], F32, l3.take(32))
        rm2 = at("rm2", [128, 8], F32, l3.take(32))
        rs12 = at("rs12", [128, 8], F32, l3.take(32))
        rmsk = at("rmsk", [128, 64], F32, l3.take(256))
        w_st = at("w_st", [128, 64], F32, l3.take(256))
        xtk = [at(f"xtk{i}", [128, 1024], F32, l3.take(4096)) for i in range(2)]
        gtmp = [at(f"gtmp{i}", [128, 1024], F32, l3.take(4096)) for i in range(2)]
        gsb = at("gsb", [128, 1024], F32, l3.take(4096))
        abf = at("abf", [128, 1024], F32, l3.take(4096))
        bbf = at("bbf", [128, 1024], F32, l3.take(4096))
        xbfp = at("xbfp", [128, 8192], BF16, l3.take(16384))
        assert l3.off <= ARENA, l3.off

        ppA = [ctx.enter_context(nc.psum_tensor(f"ppA{i}", [128, 1024], F32))
               for i in range(4)]

        ps_w = {}
        ps_r = {}

        def mm_group(keys, mms, extra_waits=()):
            waits = list(extra_waits)
            for k in keys:
                waits += ps_r.pop(k, [])
            evt = None
            nmm = len(mms)
            for i, fn in enumerate(mms):
                w = waits if i == 0 else ()
                inc = pe if i == nmm - 1 else None
                r = p.emit("pe", fn, waits=w, inc=inc)
                if r is not None:
                    evt = r
            for k in keys:
                ps_w[k] = evt
            return evt

        def ps_read(eng, fn, keys, extra_waits=()):
            waits = list(extra_waits)
            for k in keys:
                w = ps_w.get(k)
                if w is not None:
                    waits.append(w)
            sem = {"dve": dve, "act": act, "po": po}[eng]
            evt = p.emit(eng, fn, waits=waits, inc=sem)
            for k in keys:
                ps_r.setdefault(k, []).append(evt)
            return evt

        p.emit("po", lambda e: e.dma_start(out=wqkv[:], in_=wqkvp), inc=wsm, n=16)
        p.emit("po", lambda e: e.dma_start(out=gwb[:], in_=gwp), inc=wsm, n=16)
        ev_smallw = (wsm, wsm.val)
        p.emit("dve", lambda e: e.memset(ones_c[:], 1.0), inc=dve)
        ev_ones_c = (dve, dve.val)
        p.emit("dve", lambda e: e.memset(ones_f[:], 1.0), inc=dve)
        ev_ones_f = (dve, dve.val)
        p.emit("act", lambda e: e.preload_act_table(AF.Sqrt))
        p.emit("dve", lambda e: e.memset(QT[64:128, :], 0.0), inc=dve)
        p.emit("dve", lambda e: e.memset(KT[64:128, :], 0.0), inc=dve)
        p.emit("dve", lambda e: e.memset(c60[:], 60.0), inc=dve)
        ev_pad = (dve, dve.val)

        xT3 = xT.rearrange("(c q) i -> q c i", q=128)

        ev_xuse = [None, None]
        ev_xn = None
        ev_xnp = None
        ev_ab1use = None
        ev_stat_prev = None
        ev_rowsrd_prev = [[], []]
        ev_scr_prev = [None, None]
        ev_rows_out = None
        ev_p1 = None
        def emit_xld(it):
            sl = it % 2
            src = xT3[:, :, it * 512:(it + 1) * 512]
            dst = xbf1[sl][:].rearrange("q (c i) -> q c i", c=8)
            return p.emit("po", lambda e, d=dst, s=src: e.dma_start(out=d, in_=s),
                          waits=ew(ev_xuse[sl], ev_ab1use), inc=wx2[sl], n=16)

        stA = {}
        stB = {}

        def stage_A(it):
            nonlocal ev_stat_prev
            sl = it % 2
            ev_ld = emit_xld(it)
            ev_sq = p.emit("act", lambda e, a=xbf1[sl]: e.activation(
                x2b1[:], a[:], AF.Square), waits=ew(ev_ld, ev_stat_prev),
                inc=act)
            mms = []
            for ec in range(8):
                mms.append(lambda e, a=xbf1[sl], c=ec: e.matmul(
                    ppA[1][0:1, 0:512], ones_c[:], a[:, c * 512:(c + 1) * 512],
                    start=(c == 0), stop=(c == 7)))
            for ec in range(8):
                mms.append(lambda e, c=ec: e.matmul(
                    ppA[1][0:1, 512:1024], ones_c[:], x2b1[:, c * 512:(c + 1) * 512],
                    start=(c == 0), stop=(c == 7)))
            ev_stat_prev = mm_group([(1, 0), (1, 1)], mms,
                                    extra_waits=ew(ev_ld, ev_sq, ev_ones_c))
            stA[it] = ev_ld

        def stage_B(it):
            nonlocal ev_rows_out
            sl = it % 2
            scrA = scrA2[sl]
            scrB = scrB2[sl]
            rows1 = rows12[sl]
            ev_mean = ps_read("dve", lambda e, o=scrA: e.tensor_scalar_mul(
                o[:], ppA[1][0:1, 0:512], 1.0 / E), [(1, 0)])
            ev_msq = p.emit("act", lambda e, o=scrB, i_=scrA: e.activation(
                o[:], i_[:], AF.Square), waits=ew(ev_mean), inc=act)
            ev_var = ps_read("dve", lambda e, o=scrB: e.scalar_tensor_tensor(
                o[:], ppA[1][0:1, 512:1024], 1.0 / E, o[:], ALU.mult,
                ALU.subtract), [(1, 1)], extra_waits=ew(ev_msq))
            ev_vre = p.emit("dve", lambda e, o=scrB: e.tensor_scalar_add(
                o[:], o[:], 1e-5), waits=ew(ev_var), inc=dve)
            ev_sd = p.emit("act", lambda e, o=scrB: e.activation(
                o[:], o[:], AF.Sqrt), waits=ew(ev_vre), inc=act)
            ev_a = p.emit("dve", lambda e, o=rows1, i_=scrB: e.reciprocal(
                o[0:1, :], i_[:]), waits=ew(ev_sd, *ev_rowsrd_prev[sl]),
                inc=dve)
            ev_b = p.emit("dve", lambda e, o=rows1, a_=scrA: e.scalar_tensor_tensor(
                o[32:33, :], a_[:], -1.0, o[0:1, :], ALU.mult, ALU.mult),
                waits=ew(ev_a), inc=dve)
            ev_ro0 = p.emit("sp", lambda e, t=it, o=rows1: e.dma_start(
                out=rows_hbm[0:1, t * 512:(t + 1) * 512], in_=o[0:1, :]),
                waits=ew(ev_a) + [(rd, rd.val)], inc=rd, n=16)
            ev_rows_out = p.emit("sp", lambda e, t=it, o=rows1: e.dma_start(
                out=rows_hbm[1:2, t * 512:(t + 1) * 512], in_=o[32:33, :]),
                waits=ew(ev_b) + [(rd, rd.val)], inc=rd, n=16)
            ev_ab = mm_group([(2, 0)], [lambda e, o=rows1: e.matmul(
                ppA[2][:, 0:512], ones_f[0:1, :], o[0:1, :], start=True,
                stop=True)], extra_waits=ew(ev_a, ev_ones_f))
            ev_bb = mm_group([(2, 1)], [lambda e, o=rows1: e.matmul(
                ppA[2][:, 512:1024], ones_f[32:33, :], o[32:33, :], start=True,
                stop=True)], extra_waits=ew(ev_b))
            ev_rowsrd_prev[sl] = [ev_ro0, ev_rows_out, ev_ab, ev_bb]
            ps_read("act", lambda e: e.copy(
                abf[:, 0:512], ppA[2][:, 0:512]), [(2, 0)],
                extra_waits=ew(ev_ab1use, ev_xn))
            ev_evb = ps_read("act", lambda e: e.copy(
                bbf[:, 0:512], ppA[2][:, 512:1024]), [(2, 1)])
            stB[it] = ev_evb

        def stage_C(it):
            nonlocal ev_xn, ev_xnp, ev_ab1use, ev_p1
            sl = it % 2
            ev_evb = stB[it]
            for ec in range(8):
                if ec < 3:
                    em = p.emit("dve", lambda e, a=xbf1[sl], c=ec: e.tensor_tensor(
                        ttmp1[:], a[:, c * 512:(c + 1) * 512], abf[:, 0:512],
                        ALU.mult), waits=ew(ev_evb, ev_xn), inc=dve)
                    ev_xn = p.emit("dve", lambda e, c=ec: e.tensor_tensor(
                        xn1[:, c * 512:(c + 1) * 512], ttmp1[:], bbf[:, 0:512],
                        ALU.add), waits=ew(em), inc=dve)
                else:
                    em = p.emit("po", lambda e, a=xbf1[sl], c=ec: e.tensor_tensor(
                        ttmp1p[:], a[:, c * 512:(c + 1) * 512], abf[:, 0:512],
                        ALU.mult), waits=ew(ev_evb, ev_xnp), inc=po)
                    ev_xnp = p.emit("po", lambda e, c=ec: e.tensor_tensor(
                        xn1[:, c * 512:(c + 1) * 512], ttmp1p[:], bbf[:, 0:512],
                        ALU.add), waits=ew(em), inc=po)
            ev_ab1use = (po, po.val)
            ev_xuse[sl] = ev_xn
            qmm = [lambda e, c=ec: e.matmul(
                ppA[3][0:64, 0:512], wqkv[:, c * 64:(c + 1) * 64],
                xn1[:, c * 512:(c + 1) * 512], start=(c == 0), stop=(c == 7))
                for ec in range(8)]
            kmm = [lambda e, c=ec: e.matmul(
                ppA[3][0:64, 512:1024], wqkv[:, 512 + c * 64:512 + (c + 1) * 64],
                xn1[:, c * 512:(c + 1) * 512], start=(c == 0), stop=(c == 7))
                for ec in range(8)]
            mm_group([(3, 0), (3, 1)], qmm + kmm,
                     extra_waits=ew(ev_xn, ev_xnp, ev_smallw))
            vmm = []
            for g4 in range(4):
                for ec in range(8):
                    vmm.append(lambda e, g=g4, c=ec: e.matmul(
                        ppA[0][:, g * 64:(g + 1) * 64],
                        xn1[:, c * 512 + g * 128:c * 512 + (g + 1) * 128],
                        wqkv[:, 1024 + c * 64:1024 + (c + 1) * 64],
                        start=(c == 0), stop=(c == 7)))
            mm_group([(0, 0)], vmm, extra_waits=ew(ev_xn, ev_xnp))
            ps_read("act", lambda e, t=it: e.copy(
                QT[0:64, t * 512:(t + 1) * 512], ppA[3][0:64, 0:512]), [(3, 0)])
            ps_read("act", lambda e, t=it: e.copy(
                KT[0:64, t * 512:(t + 1) * 512], ppA[3][0:64, 512:1024]), [(3, 1)])
            ev_p1a = None
            for g4 in range(4):
                ev_p1a = ps_read("act", lambda e, t=it, g=g4: e.copy(
                    Vb[:, (t * 4 + g) * 64:(t * 4 + g + 1) * 64],
                    ppA[0][:, g * 64:(g + 1) * 64]), [(0, 0)])
            ev_p1 = ev_p1a

        stage_A(0)
        stage_B(0)
        stage_A(1)
        for it in range(NIT):
            stage_C(it)
            if it + 1 < NIT:
                stage_B(it + 1)
            if it + 2 < NIT:
                stage_A(it + 2)

        if taps:
            p.emit("sp", lambda e: e.dma_start(out=tap_aps["t_qt"], in_=QT[:]),
                   waits=ew(ev_p1), inc=sp_d, n=16)
            p.emit("sp", lambda e: e.dma_start(out=tap_aps["t_kt"], in_=KT[:]),
                   waits=ew(ev_p1), inc=sp_d, n=16)
            p.emit("sp", lambda e: e.dma_start(out=tap_aps["t_vb"], in_=Vb[:]),
                   waits=ew(ev_p1), inc=sp_d, n=16)
            p.emit("sp", lambda e: e.dma_start(out=tap_aps["t_rows"], in_=rows_hbm),
                   waits=ew(ev_rows_out), inc=sp_d, n=16)

        e_ks = p.emit("act", lambda e: e.activation(ksq[:], KT[0:64, :], AF.Square),
                      waits=ew(ev_p1), inc=act)
        knmm = []
        for kc in range(NKC):
            knmm.append(lambda e, k=kc: e.matmul(
                ppA[1][:, k:k + 1], ksq[:, k * 128:(k + 1) * 128],
                ones_c[0:64, :], start=True, stop=True))
        ev_knm = mm_group([(1, 0)], knmm, extra_waits=ew(e_ks, ev_ones_c))
        e_qs = p.emit("act", lambda e: e.activation(ksq[:], QT[0:64, :], AF.Square),
                      waits=ew(ev_knm), inc=act)
        ev_qn = None
        for t in range(16):
            h = t % 2
            key = (3, h)
            evm = mm_group([key], [lambda e, tt=t, hh=h: e.matmul(
                ppA[3][0:1, hh * 512:hh * 512 + 512],
                ones_c[0:64, :], ksq[:, tt * 512:(tt + 1) * 512],
                start=True, stop=True)],
                extra_waits=ew(e_qs) if t == 0 else ())
            ev_qn = ps_read("dve", lambda e, tt=t, hh=h: e.reduce_max(
                qm16[:, tt:tt + 1], ppA[3][0:1, hh * 512:hh * 512 + 512], AX.X),
                [key])
        e_qm = p.emit("dve", lambda e: e.reduce_max(qm16[:, 0:1], qm16[:], AX.X),
                      waits=ew(ev_qn), inc=dve)
        e_qsr = p.emit("act", lambda e: e.activation(qm16[:, 0:1], qm16[:, 0:1],
                                                     AF.Sqrt),
                       waits=ew(e_qm), inc=act)
        e_qb = mm_group([(2, 0)], [lambda e: e.matmul(
            ppA[2][:, 0:1], ones_f[0:1, :], qm16[:, 0:1], start=True, stop=True)],
            extra_waits=ew(e_qsr, ev_ones_f))
        e_nq = ps_read("dve", lambda e: e.tensor_scalar_mul(
            negQ[:], ppA[2][:, 0:1], -1.0), [(2, 0)])
        e_kn = ps_read("act", lambda e: e.activation(kno[:], ppA[1][:, 0:64],
                                                     AF.Sqrt), [(1, 0)],
                       extra_waits=ew())
        e_bn = p.emit("dve", lambda e: e.scalar_tensor_tensor(
            bneg[:], kno[:], negQ[:], c60[:].broadcast_to([128, 64]),
            ALU.mult, ALU.add), waits=ew(e_kn, e_nq, ev_pad), inc=dve)

        p.emit("act", lambda e: e.preload_act_table(AF.Exp))
        rot3 = [0, 1, 3]
        ev_exp = [None] * GRP
        ev_zuse = [None] * GRP
        ev_vzuse = None
        ev_sacc = {}
        pss_ap = [ppA[2][:, 0:256], ppA[2][:, 512:768]]
        rot = 0
        pending = None
        ptuse_fine = None

        def emit_burst(pend, lo=0, hi=16, fine=None):
            nonlocal ptuse_fine, ev_vzuse
            ev_expg, ev_vzg, g_ = pend
            gw_ = ew(ev_vzg, *ev_expg)
            if fine is None:
                fine = [None] * 16
            for ig4 in range(lo, hi):
                slot = ig4 % 2
                key = (2, slot)
                for sub in range(4):
                    ig = ig4 * 4 + sub
                    pp = pss_ap[slot][:, sub * 64:(sub + 1) * 64]
                    mms = [lambda e, cc=c2, i=ig, pp_=pp: e.matmul(
                        pp_, pT[cc][:, i * 128:(i + 1) * 128],
                        vzb[:, cc * 64:(cc + 1) * 64], start=(cc == 0),
                        stop=(cc == GRP - 1)) for c2 in range(GRP)]
                    if sub == 0:
                        waits = list(gw_) + ps_r.pop(key, [])
                    else:
                        waits = []
                    for i2, fn2 in enumerate(mms):
                        r2 = p.emit("pe", fn2, waits=waits if i2 == 0 else (),
                                    inc=pe if (sub == 3 and i2 == GRP - 1) else None)
                        if r2 is not None:
                            ps_w[key] = r2
                            fine[ig4] = r2
                sl4 = s_tok[:, ig4 * 256:(ig4 + 1) * 256]
                if g_ == 0:
                    ev_sacc[ig4] = ps_read("dve", lambda e, o=sl4, s_=slot:
                                           e.tensor_copy(o, pss_ap[s_]), [key])
                else:
                    ev_sacc[ig4] = ps_read("dve", lambda e, o=sl4, s_=slot:
                                           e.tensor_tensor(o, pss_ap[s_], o,
                                                           ALU.add), [key])
            if hi == 16:
                ptuse_fine = fine
                ev_vzuse = (pe, pe.val)
            return fine

        for g in range(NKC // GRP):
            e_zc = [None] * GRP
            for c in range(GRP):
                kc = g * GRP + c
                bc = bneg[:, kc:kc + 1]
                pair_keys = [None] * 8
                for tp in range(8):
                    r = rot3[rot % 3]
                    rot += 1
                    keys = [(r, 0), (r, 1)]
                    pair_keys[tp] = (keys, r)
                def emit_pair(tp):
                    keys, r = pair_keys[tp]
                    mms = [lambda e, k=kc, tt=2 * tp + hh, rr=r, hh2=hh: e.matmul(
                        ppA[rr][:, hh2 * 512:(hh2 + 1) * 512],
                        KT[:, k * 128:(k + 1) * 128],
                        QT[:, tt * 512:(tt + 1) * 512], start=True, stop=True)
                        for hh in range(2)]
                    mm_group(keys, mms, extra_waits=ew(ev_p1))

                def emit_exp(tp, war):
                    keys, r = pair_keys[tp]
                    return ps_read("act", lambda e, cc=c, tp_=tp, rr=r, b_=bc,
                                   z_=mstat[:, c * 8 + tp:c * 8 + tp + 1]:
                                   e.activation(
                        pT[cc][:, tp_ * 1024:(tp_ + 1) * 1024], ppA[rr][:],
                        AF.Exp, bias=b_, accum_out=z_), keys,
                        extra_waits=ew(e_bn, war,
                                       ev_zuse[c] if tp == 0 else None))

                for tp in range(3):
                    emit_pair(tp)
                e_x = None
                if c == 0 and pending is not None:
                    fine = emit_burst(pending, 0, 2)
                    for seg in range(8):
                        ptp = {1: 3, 2: 4, 3: 5, 4: 6, 5: 7}.get(seg)
                        if ptp is not None:
                            emit_pair(ptp)
                        e_x = emit_exp(seg, fine[2 * seg + 1])
                        if seg < 7:
                            emit_burst(pending, 2 * seg + 2, 2 * seg + 4, fine)
                    pending = None
                else:
                    for tp in range(8):
                        if tp >= 3:
                            emit_pair(tp)
                        war = (ptuse_fine[2 * tp + 1]
                               if ptuse_fine is not None else None)
                        e_x = emit_exp(tp, war)
                ev_exp[c] = e_x
                e_zc[c] = p.emit("dve", lambda e, cc=c: e.reduce_sum(
                    mstat[:, 32 + cc:32 + cc + 1],
                    mstat[:, cc * 8:cc * 8 + 8], AX.X), waits=ew(e_x), inc=dve)
                ev_zuse[c] = e_zc[c]
            e_zi = p.emit("dve", lambda e: e.reciprocal(
                mstat[:, 36:40], mstat[:, 32:36]), waits=ew(*e_zc), inc=dve)
            ev_vz = p.emit("po", lambda e, gg=g: e.tensor_tensor(
                vzb[:].rearrange("q (a b) -> q a b", b=64),
                Vb[:, gg * 256:(gg + 1) * 256].rearrange("q (a b) -> q a b", b=64),
                mstat[:, 36:40].unsqueeze(2).broadcast_to([128, 4, 64]), ALU.mult),
                waits=ew(e_zi, ev_vzuse), inc=po)
            pending = (list(ev_exp), ev_vz, g)
        emit_burst(pending)
        ev_p2_pe = (pe, pe.val)
        ev_p2_act = (act, act.val)
        if taps:
            p.emit("sp", lambda e: e.dma_start(out=tap_aps["t_stok"], in_=s_tok[:]),
                   waits=ew(ev_sacc[15]), inc=sp_d, n=16)
            p.emit("sp", lambda e: e.dma_start(out=tap_aps["t_pt0"], in_=pT[0][:]),
                   waits=ew(ev_p2_pe), inc=sp_d, n=16)
            p.emit("sp", lambda e: e.dma_start(out=tap_aps["t_mst"], in_=mstat[:, 0:32]),
                   waits=ew(ev_p2_pe), inc=sp_d, n=16)

        p.emit("act", lambda e: e.preload_act_table(AF.Silu))
        NPIECE = NST * NE * 8
        piece_ev = {}
        piece_cons = {}

        def emit_piece(n):
            if n >= NPIECE:
                return
            st_, rem = divmod(n, NE * 8)
            e_, q_ = divmod(rem, 8)
            b_ = n % 2
            w_ = ew(piece_cons.get(n - 2, ev_p2_pe))
            e1 = p.emit("po", lambda e, E_=e_, qq=q_, bb=b_: e.dma_start(
                out=w1s[bb][:], in_=w1p[E_, :, qq * 2048:(qq + 1) * 2048]),
                waits=w_, inc=w1m[b_], n=16)
            e2 = p.emit("po", lambda e, E_=e_, qq=q_, bb=b_: e.dma_start(
                out=w2s[bb][:], in_=w2p[E_, :, qq * 2048:(qq + 1) * 2048]),
                inc=w2m[b_], n=16)
            piece_ev[n] = (e1, e2)

        emit_piece(0)
        emit_piece(1)
        ev_w3use = ev_p2_act
        ev_w3use2 = ev_p2_pe
        ev_uuse = ev_p2_pe
        ev_xnuse = None
        ev_rows3use = ev_p2_pe
        ev_s1use = [ev_p2_pe, ev_p2_pe]
        ev_wstuse = ev_p2_pe
        ev_g8use = ev_p2_pe
        ev_xtkuse = [ev_p2_pe, ev_p2_pe]
        ev_gtmpuse = [ev_p2_pe, ev_p2_pe]
        ev_gsuse = ev_p2_pe
        ev_stuse = [None, None]
        ev_adprev = None
        ev_ytuse = {}
        ev_yevac = {}
        ev_w = None
        ev_xbfuse = None
        ev_xbload = None
        ev_abuse = ev_ab1use
        ev_xn3 = None
        gpre = 0

        def emit_prelude(st, defer=False):
            nonlocal ev_rows3use, ev_xn3, ev_xbfuse, ev_abuse
            ev_r3a = p.emit("sp", lambda e, t=st: e.dma_start(
                out=rows3[0:1, :], in_=rows_hbm[0:1, t * 1024:(t + 1) * 1024]),
                waits=ew(ev_rows_out, ev_rows3use) + [(rd, rd.val)], inc=rd, n=16)
            ev_r3b = p.emit("sp", lambda e, t=st: e.dma_start(
                out=rows3[32:33, :], in_=rows_hbm[1:2, t * 1024:(t + 1) * 1024]),
                waits=[(rd, rd.val)], inc=rd, n=16)
            mm_group([(0, 0), (0, 1)], [
                lambda e, h=h2: e.matmul(
                    ppA[0][:, h * 512:(h + 1) * 512], ones_f[0:1, :],
                    rows3[0:1, h * 512:(h + 1) * 512], start=True, stop=True)
                for h2 in range(2)], extra_waits=ew(ev_r3a, ev_ones_f))
            mm_group([(1, 0), (1, 1)], [
                lambda e, h=h2: e.matmul(
                    ppA[1][:, h * 512:(h + 1) * 512], ones_f[32:33, :],
                    rows3[32:33, h * 512:(h + 1) * 512], start=True, stop=True)
                for h2 in range(2)], extra_waits=ew(ev_r3b))
            ev_rows3use = (pe, pe.val)
            ps_read("dve", lambda e: e.tensor_copy(abf[:], ppA[0][:]),
                    [(0, 0), (0, 1)], extra_waits=ew(ev_abuse))
            ev_ab2 = ps_read("dve", lambda e: e.tensor_copy(bbf[:], ppA[1][:]),
                             [(1, 0), (1, 1)])
            tts = []
            for ec in range(8):
                tts.append((lambda e, c=ec: e.tensor_tensor(
                    s1b[0][:], xbfp[:, c * 1024:(c + 1) * 1024], abf[:],
                    ALU.mult),
                    ew(ev_ab2, ev_s1use[0], ev_xnuse, ev_xbload) if ec == 0
                    else (), True))
                tts.append((lambda e, c=ec: e.tensor_tensor(
                    xn3[:, c * 1024:(c + 1) * 1024], s1b[0][:], bbf[:],
                    ALU.add), (), False))
            if defer:
                return tts
            ev_xn = None
            for fn, w_, _m in tts:
                ev_xn = p.emit("dve", fn, waits=list(w_) + ew(ev_xn), inc=dve)
            ev_s1use[0] = ev_xn
            ev_xn3 = ev_xn
            ev_xbfuse = ev_xn
            ev_abuse = ev_xn
            return None

        def finish_prelude(ev_xn):
            nonlocal ev_s1use, ev_xn3, ev_xbfuse, ev_abuse
            ev_s1use[0] = ev_xn
            ev_xn3 = ev_xn
            ev_xbfuse = ev_xn
            ev_abuse = ev_xn

        def emit_xbfp_load(st):
            nonlocal ev_xbload
            src2 = xT3[:, :, st * 1024:(st + 1) * 1024]
            dst2 = xbfp[:].rearrange("q (c i) -> q c i", c=8)
            ev_xbload = p.emit("po", lambda e, d=dst2, s=src2: e.dma_start(
                out=d, in_=s), waits=ew(ev_xbfuse), inc=wx3s, n=16)
            return ev_xbload

        ev_xbfuse = None
        emit_xbfp_load(0)
        emit_prelude(0)

        for st in range(NST):
            gmm = []
            for ig in range(8):
                for ec in range(8):
                    gmm.append(lambda e, i=ig, c=ec: e.matmul(
                        ppA[2][:, i * 8:(i + 1) * 8],
                        xn3[:, c * 1024 + i * 128:c * 1024 + (i + 1) * 128],
                        gwb[:, c * 8:(c + 1) * 8], start=(c == 0), stop=(c == 7)))
            mm_group([(2, 0)], gmm, extra_waits=ew(ev_xn3, ev_smallw))
            ev_g8 = ps_read("dve", lambda e: e.tensor_copy(g8[:], ppA[2][:, 0:64]),
                            [(2, 0)], extra_waits=ew(ev_g8use, ev_wstuse))
            g83 = g8[:].rearrange("q (a b) -> q a b", b=8)
            rmsk3 = rmsk[:].rearrange("q (a b) -> q a b", b=8)
            wst3 = w_st[:].rearrange("q (a b) -> q a b", b=8)
            e_m1 = p.emit("dve", lambda e: e.reduce_max(rm1[:], g83, AX.X),
                          waits=ew(ev_g8), inc=dve)
            e_msk = p.emit("dve", lambda e: e.tensor_tensor(
                rmsk3, g83, rm1[:].unsqueeze(2).broadcast_to([128, 8, 8]),
                ALU.is_ge), waits=ew(e_m1), inc=dve)
            e_msk2 = p.emit("dve", lambda e: e.scalar_tensor_tensor(
                rmsk[:], rmsk[:], -1e30, g8[:], ALU.mult, ALU.add),
                waits=ew(e_msk), inc=dve)
            e_m2 = p.emit("dve", lambda e: e.reduce_max(rm2[:], rmsk3, AX.X),
                          waits=ew(e_msk2), inc=dve)
            e_s12 = p.emit("dve", lambda e: e.tensor_tensor(
                rs12[:], rm1[:], rm2[:], ALU.add), waits=ew(e_m2), inc=dve)
            e_t = p.emit("dve", lambda e: e.scalar_tensor_tensor(
                rmsk3, g83, 2.0,
                rs12[:].unsqueeze(2).broadcast_to([128, 8, 8]),
                ALU.mult, ALU.subtract), waits=ew(e_s12), inc=dve)
            e_sg = p.emit("act", lambda e: e.activation(
                rmsk[:], rmsk[:], AF.Tanh, scale=0.5), waits=ew(e_t), inc=act)
            e_sg2 = p.emit("dve", lambda e: e.tensor_scalar(
                rmsk[:], rmsk[:], 0.5, 0.5, ALU.mult, ALU.add),
                waits=ew(e_sg), inc=dve)
            e_ch = p.emit("dve", lambda e: e.tensor_tensor(
                wst3, g83, rm2[:].unsqueeze(2).broadcast_to([128, 8, 8]),
                ALU.is_ge), waits=ew(e_sg2), inc=dve)
            ev_w = p.emit("dve", lambda e: e.tensor_tensor(
                w_st[:], w_st[:], rmsk[:], ALU.mult), waits=ew(e_ch), inc=dve)
            ev_g8use = ev_w
            if taps and st == 0:
                p.emit("sp", lambda e: e.dma_start(out=tap_aps["t_g8"], in_=g8[:]),
                       waits=ew(ev_g8), inc=sp_d, n=16)
                p.emit("sp", lambda e: e.dma_start(out=tap_aps["t_wst"], in_=w_st[:]),
                       waits=ew(ev_w), inc=sp_d, n=16)
                p.emit("sp", lambda e: e.dma_start(out=tap_aps["t_xn"], in_=xn3[:]),
                       waits=ew(ev_xn3), inc=sp_d, n=16)
            if st + 1 < NST:
                emit_xbfp_load(st + 1)

            for ee in range(NE):
                ev_w3ld = None
                for q in range(4):
                    ev_w3ld = p.emit("po", lambda e, E_=ee, qq=q: e.dma_start(
                        out=w3s[:, qq * 4096:(qq + 1) * 4096],
                        in_=w3p[E_, :, qq * 4096:(qq + 1) * 4096]),
                        waits=ew(ev_w3use, ev_w3use2), inc=w3m, n=16)
                ev_w3use = None
                ev_utt = {}
                for j in range(16):
                    jp = j % 2
                    qq = j // 2
                    npc = gpre + qq
                    b = npc % 2
                    t1, t2 = 2 * jp, 2 * jp + 1
                    extra = ew(*piece_ev[npc])
                    if ee == 0 and j == 0:
                        extra += ew(ev_xn3)
                    mms = []
                    for ec in range(8):
                        for ih in range(2):
                            mms.append(lambda e, c=ec, h=ih, t=t1, jj=j, bb=b:
                                       e.matmul(
                                ppA[t][:, h * 512:(h + 1) * 512],
                                w1s[bb][:, (jj % 2) * 1024 + c * 128:
                                         (jj % 2) * 1024 + (c + 1) * 128],
                                xn3[:, c * 1024 + h * 512:c * 1024 + (h + 1) * 512],
                                start=(c == 0), stop=(c == 7)))
                    mm_group([(t1, 0), (t1, 1)], mms, extra_waits=extra)
                    mms = []
                    for ec in range(8):
                        for ih in range(2):
                            mms.append(lambda e, c=ec, h=ih, t=t2, jj=j, bb=b:
                                       e.matmul(
                                ppA[t][:, h * 512:(h + 1) * 512],
                                w2s[bb][:, (jj % 2) * 1024 + c * 128:
                                         (jj % 2) * 1024 + (c + 1) * 128],
                                xn3[:, c * 1024 + h * 512:c * 1024 + (h + 1) * 512],
                                start=(c == 0), stop=(c == 7)))
                    mm_group([(t2, 0), (t2, 1)], mms)
                    if j == 15:
                        ev_xnuse = (pe, pe.val)
                    if j % 2 == 1:
                        piece_cons[npc] = (pe, pe.val)
                        emit_piece(npc + 2)
                    ev_s1 = p.emit("act", lambda e, jp_=jp, t=t1: e.activation(
                        s1b[jp_][:], ppA[t][:], AF.Silu),
                        waits=ew(ps_w.get((t1, 0)), ev_s1use[jp]), inc=act)
                    ev_utt[j] = p.emit("dve", lambda e, jj=j, jp_=jp, t=t2:
                                       e.tensor_tensor(
                        u_buf[:, jj * 1024:(jj + 1) * 1024], s1b[jp_][:],
                        ppA[t][:], ALU.mult),
                        waits=ew(ps_w.get((t2, 0)), ev_s1), inc=dve)
                    for k_ in ((t1, 0), (t1, 1), (t2, 0), (t2, 1)):
                        ps_r.setdefault(k_, []).append(ev_utt[j])
                    ev_s1use[jp] = ev_utt[j]
                gpre += 8
                ptts = None
                ev_xnl = None
                if ee == NE - 1 and st + 1 < NST:
                    ptts = emit_prelude(st + 1, defer=True)
                for ig in range(8):
                    t = ig % 4
                    keysB = [(t, 0), (t, 1)]
                    bw = list(ew(ev_w3ld))
                    for k in keysB:
                        bw += ps_r.pop(k, [])
                    evB = None
                    for j in range(16):
                        for eh in range(2):
                            w_ = (bw if (j == 0 and eh == 0) else
                                  (ew(ev_utt[j]) if (ig == 0 and eh == 0)
                                   else ()))
                            if j == 0 and eh == 0:
                                w_ = bw + ew(ev_utt[0])
                            r3 = p.emit("pe", lambda e, jj=j, h=eh, t_=t, i=ig:
                                        e.matmul(
                                ppA[t_][:, h * 512:(h + 1) * 512],
                                u_buf[:, jj * 1024 + i * 128:
                                      jj * 1024 + (i + 1) * 128],
                                w3s[:, jj * 1024 + h * 512:
                                    jj * 1024 + (h + 1) * 512],
                                start=(jj == 0), stop=(jj == 15)), waits=w_,
                                inc=pe if (j == 15 and eh == 1) else None)
                            if r3 is not None:
                                evB = r3
                    for k in keysB:
                        ps_w[k] = evB
                    if ig == 7:
                        ev_uuse = (pe, pe.val)
                        ev_w3use2 = (pe, pe.val)
                    for eh in range(2):
                        ysl = y_tot[:, ig * 1024 + eh * 512:
                                    ig * 1024 + (eh + 1) * 512]
                        yps = ppA[t][:, eh * 512:(eh + 1) * 512]
                        wcol = (w_st[:, ig * 8 + ee:ig * 8 + ee + 1]
                                if ee < 8 else None)
                        if ee == 0:
                            ev_yevac[(ig, eh)] = ps_read(
                                "dve", lambda e, o=ysl, i_=yps, w=wcol:
                                e.tensor_scalar_mul(o, i_, w), [(t, eh)],
                                extra_waits=ew(ev_w, ev_ytuse.get(ig)))
                        elif ee == NE - 1:
                            ev_yevac[(ig, eh)] = ps_read(
                                "dve", lambda e, o=ysl, i_=yps:
                                e.scalar_tensor_tensor(o, i_, 0.0, o, ALU.bypass,
                                                       ALU.add), [(t, eh)])
                        else:
                            ev_yevac[(ig, eh)] = ps_read(
                                "dve", lambda e, o=ysl, i_=yps, w=wcol:
                                e.scalar_tensor_tensor(o, i_, w, o, ALU.mult,
                                                       ALU.add), [(t, eh)],
                                extra_waits=ew(ev_w))
                    if ptts is not None:
                        for fn, w_, _m in ptts[ig * 2:ig * 2 + 2]:
                            ev_xnl = p.emit("dve", fn,
                                            waits=list(w_) + ew(ev_xnl),
                                            inc=dve)
                        if ig == 7:
                            finish_prelude(ev_xnl)
                    if ee == NE - 1:
                        sl = ig % 2
                        gig = st * 8 + ig
                        xsrc = xtok.rearrange("(g q) d -> g q d", q=128)[gig]
                        odst = out.rearrange("(g q) d -> g q d", q=128)[gig]
                        ev_xt = p.emit("sp", lambda e, s_=sl, src_=xsrc:
                                       e.dma_start(out=xtk[s_][:], in_=src_),
                                       waits=ew(ev_xtkuse[sl]), inc=xk2[sl], n=16)
                        e_si = p.emit("act", lambda e, i=ig, s_=sl: e.activation(
                            gtmp[s_][:], y_tot[:, i * 1024:(i + 1) * 1024],
                            AF.Silu),
                            waits=ew(ev_yevac[(ig, 0)], ev_yevac[(ig, 1)],
                                     ev_gtmpuse[sl], ev_stuse[sl]), inc=act)
                        ev_ytuse[ig] = e_si
                        e_gs = p.emit("dve", lambda e, g_=gig, s_=sl:
                                      e.tensor_tensor(
                            gsb[:].rearrange("q (r d) -> q r d", d=64),
                            gtmp[s_][:].rearrange("q (r d) -> q r d", d=64),
                            s_tok[:, g_ * 64:(g_ + 1) * 64].unsqueeze(1)
                            .broadcast_to([128, 16, 64]), ALU.mult),
                            waits=ew(e_si, ev_adprev,
                                     ev_sacc.get(gig // 4)
                                     if st == 0 else None), inc=dve)
                        ev_gtmpuse[sl] = e_gs
                        e_ad = p.emit("dve", lambda e, s_=sl: e.tensor_tensor(
                            gtmp[s_][:], gsb[:], xtk[s_][:], ALU.add),
                            waits=ew(ev_xt, e_gs), inc=dve)
                        ev_xtkuse[sl] = e_ad
                        ev_adprev = e_ad
                        ev_st2 = p.emit("sp", lambda e, s_=sl, od_=odst:
                                        e.dma_start(out=od_, in_=gtmp[s_][:]),
                                        waits=ew(e_ad), inc=od2[sl], n=16)
                        ev_stuse[sl] = ev_st2
                ev_wstuse = (dve, dve.val)
            if taps and st == 0:
                p.emit("sp", lambda e: e.dma_start(out=tap_aps["t_ytot"],
                                                   in_=y_tot[:]),
                       waits=ew(ev_yevac[(7, 1)]), inc=sp_d, n=16)
                p.emit("sp", lambda e: e.dma_start(out=tap_aps["t_u"], in_=u_buf[:]),
                       waits=ew(ev_uuse), inc=sp_d, n=16)

        p.emit("sp", lambda e: e.nop(),
               waits=[(od2[0], od2[0].val), (od2[1], od2[1].val),
                      (rd, rd.val), (sp_d, sp_d.val)])

        for s in p.sems:
            s.h = ctx.enter_context(nc.semaphore(s.name))
        block = ctx.enter_context(nc.Block())
        block.sync(lambda e: _replay(p, "sp", e))
        block.tensor(lambda e: _replay(p, "pe", e))
        block.scalar(lambda e: _replay(p, "act", e))
        block.vector(lambda e: _replay(p, "dve", e))
        block.gpsimd(lambda e: _replay(p, "po", e))
    return nc


def _pack_weights(ew1, ew2, ew3, sw1, sw2, sw3):
    w1 = np.concatenate([ew1, sw1[None]], 0)
    w2 = np.concatenate([ew2, sw2[None]], 0)
    w3 = np.concatenate([ew3, sw3[None]], 0)
    w1p = np.ascontiguousarray(
        w1.reshape(NE, 8, 128, 16, 128).transpose(0, 2, 3, 1, 4).reshape(
            NE, 128, 16384), dtype=np.float32)
    w2p = np.ascontiguousarray(
        w2.reshape(NE, 8, 128, 16, 128).transpose(0, 2, 3, 1, 4).reshape(
            NE, 128, 16384), dtype=np.float32)
    w3p = np.ascontiguousarray(
        w3.reshape(NE, 16, 128, 1024).transpose(0, 2, 1, 3).reshape(
            NE, 128, 16384), dtype=np.float32)
    return w1p, w2p, w3p


def _pack_proj(Wq, Wk, Wv, gate_w):
    def pk(w):
        return w.T.reshape(8, 128, 64).transpose(1, 0, 2).reshape(128, 512)
    wqkvp = np.ascontiguousarray(
        np.concatenate([pk(Wq), pk(Wk), pk(Wv)], axis=1), dtype=np.float32)
    gwp = np.ascontiguousarray(
        gate_w.reshape(8, 128, 8).transpose(1, 0, 2).reshape(128, 64),
        dtype=np.float32)
    return wqkvp, gwp


def kernel(L0_attn_x, L0_attn_Wq, L0_attn_Wk, L0_attn_Wv, L0_ffn_gate_w,
           L0_ffn_ew1, L0_ffn_ew2, L0_ffn_ew3, L0_ffn_sw1, L0_ffn_sw2,
           L0_ffn_sw3):
    x = np.asarray(L0_attn_x, dtype=np.float32)
    w1p, w2p, w3p = _pack_weights(
        np.asarray(L0_ffn_ew1, np.float32), np.asarray(L0_ffn_ew2, np.float32),
        np.asarray(L0_ffn_ew3, np.float32), np.asarray(L0_ffn_sw1, np.float32),
        np.asarray(L0_ffn_sw2, np.float32), np.asarray(L0_ffn_sw3, np.float32))
    wqkvp, gwp = _pack_proj(
        np.asarray(L0_attn_Wq, np.float32), np.asarray(L0_attn_Wk, np.float32),
        np.asarray(L0_attn_Wv, np.float32), np.asarray(L0_ffn_gate_w, np.float32))

    nc = build(taps=False)
    in_maps = []
    for b in range(NCORES):
        in_maps.append({
            "xT": np.ascontiguousarray(x[b].T),
            "xtok": np.ascontiguousarray(x[b]),
            "wqkvp": wqkvp, "gwp": gwp,
            "w1p": w1p, "w2p": w2p, "w3p": w3p,
        })
    res = run_bass_kernel_spmd(nc, in_maps, core_ids=list(range(NCORES)))
    return np.stack([np.asarray(r["out"]) for r in res.results], axis=0)
